# version 41
# baseline (speedup 1.0000x reference)
import numpy as np
from contextlib import ExitStack

import concourse.bass as bass
import concourse.mybir as mybir
from concourse.bass_utils import run_bass_kernel_spmd

F32 = mybir.dt.float32
BF16 = mybir.dt.bfloat16
AF = mybir.ActivationFunctionType
ALU = mybir.AluOpType

D = 1024
H = 16
FF = 4096
KD = D // 128
KF = FF // 128
NCORES = 8
EPS = 1e-5

ENGS = ("tensor", "scalar", "vector", "sync", "gpsimd")


class Sem:
    def __init__(self, ctx, nc, name, step=1):
        self.h = ctx.enter_context(nc.semaphore(name))
        self.n = 0
        self.step = step
        self.name = name


class Buf:

    def __init__(self):
        self.w = None
        self.rs = []

    def read_waits(self):
        return [self.w] if self.w else []

    def write_waits(self):
        return list(self.rs) + ([self.w] if self.w else [])

    def wrote(self, tok):
        self.w = tok
        self.rs = []

    def read(self, tok):
        if tok is not None:
            self.rs.append(tok)


class Prog:

    def __init__(self):
        self.ev = {e: [] for e in ENGS}

    def add(self, eng, fn, waits=(), incs=()):
        incs = tuple(incs)
        toks = []
        for s in incs:
            s.n += s.step
            toks.append(s.n)
        w = [(s, v) for (s, v) in waits if v is not None and v > 0]
        self.ev[eng].append((fn, w, incs, toks))
        if len(incs) == 1:
            return (incs[0], toks[0])
        return None

    def emit(self, nc, waited):
        with nc.Block() as block:
            for engn in ENGS:
                evs = self.ev[engn]
                if not evs:
                    continue

                def body(eng, evs=evs, wd=waited[engn]):
                    for fn, waits, incs, toks in evs:
                        for (s, v) in waits:
                            if wd.get(s.name, 0) < v:
                                eng.wait_ge(s.h, v)
                                wd[s.name] = v
                        ins = fn(eng)
                        for s, t in zip(incs, toks):
                            ins = ins.then_inc(s.h, s.step)

                getattr(block, engn)(body)


def build_program(T):
    NTT = T // 512
    NQ = T // 1024
    TKT = T // 128
    nc = bass.Bass("TRN2", target_bir_lowering=False)

    x_f = nc.dram_tensor("x_f", [D, T], F32, kind="ExternalInput").ap()
    wqkv_s = nc.dram_tensor("wqkv_s", [3 * KD, 128, D], F32, kind="ExternalInput").ap()
    wout_s = nc.dram_tensor("wout_s", [KD, 128, D], F32, kind="ExternalInput").ap()
    bout_s = nc.dram_tensor("bout_s", [128, KD], F32, kind="ExternalInput").ap()
    w1_s = nc.dram_tensor("w1_s", [KF, 128, D], F32, kind="ExternalInput").ap()
    w2_s = nc.dram_tensor("w2_s", [KF, 128, D], F32, kind="ExternalInput").ap()
    h1_s = nc.dram_tensor("h1_s", [KF, 128, FF], F32, kind="ExternalInput").ap()
    h2_s = nc.dram_tensor("h2_s", [KF, 128, FF], F32, kind="ExternalInput").ap()
    w3_s = nc.dram_tensor("w3_s", [KD, 128, FF], F32, kind="ExternalInput").ap()
    s1_v = nc.dram_tensor("s1_v", [128, KF], F32, kind="ExternalInput").ap()
    s2_v = nc.dram_tensor("s2_v", [128, KF], F32, kind="ExternalInput").ap()
    h1s_v = nc.dram_tensor("h1s_v", [128, KF], F32, kind="ExternalInput").ap()
    h2s_v = nc.dram_tensor("h2s_v", [128, KF], F32, kind="ExternalInput").ap()
    w3s_v = nc.dram_tensor("w3s_v", [128, KD], F32, kind="ExternalInput").ap()
    out_f = nc.dram_tensor("out_f", [D, T], F32, kind="ExternalOutput").ap()

    q_sc = nc.dram_tensor("q_sc", [KD, 128, T], BF16, kind="Internal").ap()
    k_sc = nc.dram_tensor("k_sc", [KD, 128, T], BF16, kind="Internal").ap()
    x1_sc = nc.dram_tensor("x1_sc", [KD, 128, T], F32, kind="Internal").ap()
    ln2_sc = nc.dram_tensor("ln2_sc", [KD, 128, T], BF16, kind="Internal").ap()

    x_3d = x_f.rearrange("(c p) t -> p c t", p=128)
    x1_3d = x1_sc.rearrange("c p t -> p c t")
    out_3d = out_f.rearrange("(c p) t -> p c t", p=128)
    q_3d = q_sc.rearrange("c p t -> p c t")
    k_3d = k_sc.rearrange("c p t -> p c t")
    ln2_3d = ln2_sc.rearrange("c p t -> p c t")

    with ExitStack() as octx:
        DMA_SEMS = {}
        for nm in ("cst", "wq", "wo", "xt0", "xt1", "qb0", "qb1", "kb0", "kb1",
                   "ln2q", "x1r", "stg0a", "stg0b", "stg1a", "stg1b",
                   "spq", "spk", "spx1", "spln2", "spout"):
            DMA_SEMS[nm] = Sem(octx, nc, "d_" + nm, 16)
        sPE = Sem(octx, nc, "sPE")
        sACT = Sem(octx, nc, "sACT")
        sDVE = Sem(octx, nc, "sDVE")
        sPh = Sem(octx, nc, "sPh")
        waited = {e: {} for e in ENGS}

        ones_f = octx.enter_context(nc.sbuf_tensor("ones_f", [128, 1], F32))
        ones_b = octx.enter_context(nc.sbuf_tensor("ones_b", [128, 1], BF16))
        ones_row = octx.enter_context(nc.sbuf_tensor("ones_row", [1, 128], F32))
        ones_row_b = octx.enter_context(nc.sbuf_tensor("ones_row_b", [1, 128], BF16))
        bout_sb = octx.enter_context(nc.sbuf_tensor("bout_sb", [128, KD], F32))
        s1_sb = octx.enter_context(nc.sbuf_tensor("s1_sb", [128, KF], F32))
        s2_sb = octx.enter_context(nc.sbuf_tensor("s2_sb", [128, KF], F32))
        h1s_sb = octx.enter_context(nc.sbuf_tensor("h1s_sb", [128, KF], F32))
        h2s_sb = octx.enter_context(nc.sbuf_tensor("h2s_sb", [128, KF], F32))
        w3s_sb = octx.enter_context(nc.sbuf_tensor("w3s_sb", [128, KD], F32))
        row_m = octx.enter_context(nc.sbuf_tensor("row_m", [1, 512], F32))
        row_v = octx.enter_context(nc.sbuf_tensor("row_v", [1, 512], F32))
        row_r = octx.enter_context(nc.sbuf_tensor("row_r", [1, 512], F32))
        tmp_sb = octx.enter_context(nc.sbuf_tensor("tmp_sb", [128, 512], F32))
        tmp2_sb = octx.enter_context(nc.sbuf_tensor("tmp2_sb", [128, 512], F32))

        psA = octx.enter_context(nc.psum_tensor("psA", [128, 1024], F32))
        psB = octx.enter_context(nc.psum_tensor("psB", [128, 1024], F32))
        psC = octx.enter_context(nc.psum_tensor("psC", [128, 1024], F32))
        ps4 = octx.enter_context(nc.psum_tensor("ps4", [128, 512], F32))
        ps5 = octx.enter_context(nc.psum_tensor("ps5", [128, 512], F32))
        psA0, psA1 = psA[:, 0:512], psA[:, 512:1024]
        psB0, psB1 = psB[:, 0:512], psB[:, 512:1024]
        psC0, psC1 = psC[:, 0:512], psC[:, 512:1024]

        bufs = {}

        def B(name):
            if name not in bufs:
                bufs[name] = Buf()
            return bufs[name]

        P = Prog()

        def dma(queue_eng, dsem, out_ap, in_ap, rbufs=(), wbufs=()):
            waits = []
            for b in rbufs:
                waits += b.read_waits()
            for b in wbufs:
                waits += b.write_waits()
            tok = P.add(queue_eng, lambda e, o=out_ap, i=in_ap: e.dma_start(out=o, in_=i),
                        waits=waits, incs=[dsem])
            for b in rbufs:
                b.read(tok)
            for b in wbufs:
                b.wrote(tok)
            return tok

        def op(eng, sem, fn, rbufs=(), wbufs=(), extra_waits=()):
            waits = list(extra_waits)
            for b in rbufs:
                waits += b.read_waits()
            for b in wbufs:
                waits += b.write_waits()
            tok = P.add(eng, fn, waits=waits, incs=[sem])
            for b in rbufs:
                b.read(tok)
            for b in wbufs:
                b.wrote(tok)
            return tok

        def mm_group(items, out_buf, rbufs, extra_waits=(), tp=None, at_waits=None):
            waits = list(extra_waits)
            for b in rbufs:
                waits += b.read_waits()
            waits += out_buf.write_waits()
            n = len(items)
            tok = None
            for i, (o, l, r) in enumerate(items):
                st, sp = (i == 0), (i == n - 1)
                w_i = list(waits) if i == 0 else []
                if at_waits and i in at_waits:
                    w_i += [w for w in at_waits[i] if w is not None]
                tok = P.add("tensor",
                            lambda e, o=o, l=l, r=r, st=st, sp=sp, tp=tp: e.matmul(
                                o, l, r, start=st, stop=sp, tile_position=tp),
                            waits=w_i, incs=[sPE] if sp else [])
            for b in rbufs:
                b.read(tok)
            out_buf.wrote(tok)
            return tok

        def mm_group2(items2, out_a, out_b, rbufs, extra_waits=(), at_waits=None):
            waits = list(extra_waits)
            for b in rbufs:
                waits += b.read_waits()
            waits += out_a.write_waits() + out_b.write_waits()
            n = len(items2)
            tok_a = tok_b = None
            for i, (oa, ob, l, ra, rb) in enumerate(items2):
                st, sp = (i == 0), (i == n - 1)
                w_i = list(waits) if i == 0 else []
                if at_waits and i in at_waits:
                    w_i += [w for w in at_waits[i] if w is not None]
                tok_a = P.add("tensor",
                              lambda e, o=oa, l=l, r=ra, st=st, sp=sp: e.matmul(
                                  o, l, r, start=st, stop=sp, skip_group_check=True),
                              waits=w_i,
                              incs=[sPE] if sp else [])
                tok_b = P.add("tensor",
                              lambda e, o=ob, l=l, r=rb, st=st, sp=sp: e.matmul(
                                  o, l, r, start=st, stop=sp, skip_group_check=True),
                              waits=(), incs=[sPE] if sp else [])
            for b in rbufs:
                b.read(tok_b)
            out_a.wrote(tok_a)
            out_b.wrote(tok_b)
            return tok_a, tok_b

        def barrier():
            dma_totals = [(sm, sm.n) for sm in DMA_SEMS.values() if sm.n > 0]
            for eng in ENGS:
                if eng == "sync":
                    P.add(eng, lambda e: e.nop(), waits=dma_totals, incs=[sPh])
                else:
                    P.add(eng, lambda e: e.nop(), incs=[sPh])
            tgt = sPh.n
            for eng in ENGS:
                P.add(eng, lambda e: e.nop(), waits=[(sPh, tgt)])

        def layer_norm(z_sb, z_buf, sq_ap, sq_buf, out_fn, residual=None, ps_cfg=None,
                       zb=None, pre_done=False):
            if ps_cfg is None:
                stat_ps, statB0, statB1, bcm_ps, bcmB, bcr_ps, bcrB = (
                    psA0, "psA0", "psA0_hi", psA1, "psA1", psB0, "psB0")
            else:
                stat_ps, statB0, statB1, bcm_ps, bcmB, bcr_ps, bcrB = ps_cfg
            sqb = sq_buf
            if not pre_done:
                op("scalar", sACT,
                   lambda e: e.activation(out=sq_ap, in_=z_sb[:], func=AF.Square),
                   rbufs=[z_buf], wbufs=[sqb])
            stat = B(statB0)
            if zb == "self":
                mm_group([(stat_ps[0:1, :], ones_b[:], z_sb[:, c * 512:(c + 1) * 512])
                          for c in range(KD)], stat, [z_buf, B("ones")])
            elif zb is not None:
                zb_ap, zb_buf = zb
                if not pre_done:
                    op("vector", sDVE,
                       lambda e: e.tensor_copy(out=zb_ap, in_=z_sb[:]),
                       rbufs=[z_buf], wbufs=[zb_buf])
                mm_group([(stat_ps[0:1, :], ones_b[:], zb_ap[:, c * 512:(c + 1) * 512])
                          for c in range(KD)], stat, [zb_buf, B("ones")])
            else:
                mm_group([(stat_ps[0:1, :], ones_f[:], z_sb[:, c * 512:(c + 1) * 512])
                          for c in range(KD)], stat, [z_buf, B("ones")])
            stat2 = B(statB1)
            mm_group([(stat_ps[64:65, :], ones_b[:], sq_ap[:, c * 512:(c + 1) * 512]) for c in range(KD)],
                     stat2, [sqb, B("ones")])
            rm, rv, rr = B("row_m"), B("row_v"), B("row_r")
            op("vector", sDVE,
               lambda e: e.tensor_scalar(out=row_m[:], in0=stat_ps[0:1, :], scalar1=1.0 / D,
                                         scalar2=None, op0=ALU.mult),
               rbufs=[stat], wbufs=[rm])
            op("vector", sDVE,
               lambda e: e.tensor_tensor(out=row_v[:], in0=row_m[:], in1=row_m[:], op=ALU.mult),
               rbufs=[rm], wbufs=[rv])
            op("vector", sDVE,
               lambda e: e.scalar_tensor_tensor(out=row_v[:], in0=stat_ps[64:65, :], scalar=1.0 / D,
                                                in1=row_v[:], op0=ALU.mult, op1=ALU.subtract),
               rbufs=[stat2, rv], wbufs=[rv])
            op("vector", sDVE,
               lambda e: e.tensor_scalar(out=row_v[:], in0=row_v[:], scalar1=EPS, scalar2=None,
                                         op0=ALU.add),
               rbufs=[rv], wbufs=[rv])
            op("scalar", sACT,
               lambda e: e.activation(out=row_r[:], in_=row_v[:], func=AF.Ln),
               rbufs=[rv], wbufs=[rr])
            op("scalar", sACT,
               lambda e: e.activation(out=row_r[:], in_=row_r[:], func=AF.Exp, scale=-0.5),
               rbufs=[rr], wbufs=[rr])
            bcm, bcr = B(bcmB), B(bcrB)
            mm_group([(bcm_ps, ones_row[:], row_m[:])], bcm, [rm, B("ones")],
                     extra_waits=stat.write_waits() + stat2.write_waits())
            mm_group([(bcr_ps, ones_row[:], row_r[:])], bcr, [rr, B("ones")])
            tb = B("tmp_sb")
            tb2 = B("tmp2_sb")
            for c in range(KD):
                zc = z_sb[:, c * 512:(c + 1) * 512]
                op("vector", sDVE,
                   lambda e, zc=zc: e.tensor_tensor(out=tmp_sb[:], in0=zc, in1=bcm_ps,
                                                    op=ALU.subtract),
                   rbufs=[z_buf, bcm], wbufs=[tb])
                out_ap, out_b = out_fn(c)
                if residual is None:
                    op("vector", sDVE,
                       lambda e, o=out_ap: e.tensor_tensor(out=o, in0=tmp_sb[:], in1=bcr_ps,
                                                           op=ALU.mult),
                       rbufs=[tb, bcr], wbufs=[out_b])
                else:
                    res_fn, res_buf = residual
                    op("vector", sDVE,
                       lambda e: e.tensor_tensor(out=tmp2_sb[:], in0=tmp_sb[:], in1=bcr_ps,
                                                 op=ALU.mult),
                       rbufs=[tb, bcr], wbufs=[tb2])
                    op("vector", sDVE,
                       lambda e, o=out_ap, r=res_fn(c): e.tensor_tensor(out=o, in0=tmp2_sb[:],
                                                                        in1=r, op=ALU.add),
                       rbufs=[tb2, res_buf], wbufs=[out_b])

        with ExitStack() as ctxA:
            v_sb = ctxA.enter_context(nc.sbuf_tensor("v_sb", [128, TKT * H * 65], BF16))
            lnb0 = ctxA.enter_context(nc.sbuf_tensor("lnb0", [128, KD * 512], BF16))
            sq_sb = ctxA.enter_context(nc.sbuf_tensor("sq_sb", [128, KD * 512], BF16))
            xt = [ctxA.enter_context(nc.sbuf_tensor(f"xt{i}", [128, KD * 512], F32))
                  for i in range(2)]
            v4 = v_sb[:].rearrange("p (tb h j) -> p tb h j", h=H, j=65)

            with ExitStack() as ctxB1:
                wqkv_sb = ctxB1.enter_context(nc.sbuf_tensor("wqkv_sb", [128, 24 * 1024], BF16))
                lnb = [lnb0, ctxB1.enter_context(nc.sbuf_tensor("lnb1", [128, KD * 512], BF16))]
                qt2 = [ctxB1.enter_context(nc.sbuf_tensor(f"qt{i}", [128, KD * 512], BF16))
                       for i in range(2)]
                kt_s = ctxB1.enter_context(nc.sbuf_tensor("kt", [128, KD * 512], BF16))
                wv = wqkv_sb[:].rearrange("p (oj x) -> p oj x", x=1024)

                op("vector", sDVE, lambda e: e.memset(ones_f[:], 1.0), wbufs=[B("ones")])
                op("vector", sDVE, lambda e: e.memset(ones_b[:], 1.0), wbufs=[B("ones")])
                op("vector", sDVE, lambda e: e.memset(ones_row[:], 1.0), wbufs=[B("ones")])
                op("vector", sDVE, lambda e: e.memset(ones_row_b[:], 1.0), wbufs=[B("ones")])
                op("vector", sDVE, lambda e: e.memset(v4[:, :, :, 64:65], 1.0), wbufs=[B("v_ones")])
                sCst = DMA_SEMS["cst"]
                dma("sync", sCst, bout_sb[:], bout_s, wbufs=[B("bout")])
                dma("sync", sCst, s1_sb[:], s1_v, wbufs=[B("s1")])
                dma("sync", sCst, s2_sb[:], s2_v, wbufs=[B("s2")])
                dma("sync", sCst, h1s_sb[:], h1s_v, wbufs=[B("h1s")])
                dma("sync", sCst, h2s_sb[:], h2s_v, wbufs=[B("h2s")])
                dma("sync", sCst, w3s_sb[:], w3s_v, wbufs=[B("w3s")])
                for nm in ("bout", "s1", "s2", "h1s", "h2s", "w3s"):
                    B(nm).w = (sCst, sCst.n)
                sWq = DMA_SEMS["wq"]
                for oj in range(24):
                    dma("gpsimd", sWq, wqkv_sb[:, oj * 1024:(oj + 1) * 1024], wqkv_s[oj],
                        wbufs=[B(f"wqkv{oj}")])
                for oj in range(24):
                    B(f"wqkv{oj}").w = (sWq, sWq.n)

                def emit_ln1(j):
                    xb = xt[j % 2]
                    xbuf = B(f"xt{j % 2}")
                    lb = lnb[j % 2]
                    lnbuf = B(f"lnb{j % 2}")
                    dma("sync", DMA_SEMS[f"xt{j % 2}"], xb[:].rearrange("p (c t) -> p c t", t=512),
                        x_3d[:, :, j * 512:(j + 1) * 512], wbufs=[xbuf])
                    layer_norm(xb, xbuf, sq_sb[:], B("sq1"),
                               lambda c: (lb[:, c * 512:(c + 1) * 512], lnbuf),
                               zb=(lb[:], lnbuf))

                def emit_qk(j):
                    qt, kt = qt2[j % 2], kt_s
                    qtb, ktb = B(f"qt{j % 2}"), B("kt")
                    lb = lnb[j % 2]
                    lnbuf = B(f"lnb{j % 2}")
                    for oc in range(16):
                        psx, psxb = (ps4, B("ps4")) if oc % 2 == 0 else (ps5, B("ps5"))
                        mm_group([(psx[:],
                                   wqkv_sb[:, oc * 1024 + k * 128: oc * 1024 + (k + 1) * 128],
                                   lb[:, k * 512:(k + 1) * 512]) for k in range(KD)],
                                 psxb, [lnbuf, B(f"wqkv{oc}")])
                        dst, dbuf = (qt, qtb) if oc < 8 else (kt, ktb)
                        pc = oc % 8
                        op("scalar", sACT,
                           lambda e, d=dst, pc=pc, ps=psx: e.copy(
                               out=d[:, pc * 512:(pc + 1) * 512], in_=ps[:]),
                           rbufs=[psxb], wbufs=[dbuf])
                    dma("gpsimd", DMA_SEMS["spq"], q_3d[:, :, j * 512:(j + 1) * 512],
                        qt[:].rearrange("p (pc t) -> p pc t", t=512), rbufs=[qtb])
                    dma("gpsimd", DMA_SEMS["spk"], k_3d[:, :, j * 512:(j + 1) * 512],
                        kt[:].rearrange("p (pc t) -> p pc t", t=512), rbufs=[ktb])

                def emit_v(j):
                    lb = lnb[j % 2]
                    lnbuf = B(f"lnb{j % 2}")
                    for t4 in range(4):
                        tb_ix = j * 4 + t4
                        for g in range(2):
                            psv, psvb = (psC0, B("psC0")) if g == 0 else (psC1, B("psC1"))
                            items = []
                            for k in range(KD):
                                lhsT = lb[:, k * 512 + t4 * 128: k * 512 + t4 * 128 + 128]
                                rhs = wv[:, 16 + 4 * g:16 + 4 * g + 4, k * 128:(k + 1) * 128]
                                items.append((psv[:], lhsT, rhs))
                            mm_group(items, psvb,
                                     [lnbuf] + [B(f"wqkv{16 + 4 * g + i}") for i in range(4)])
                            op("vector", sDVE,
                               lambda e, o=v4[:, tb_ix, 8 * g:8 * g + 8, 0:64], ps=psv:
                               e.tensor_copy(out=o, in_=ps),
                               rbufs=[psvb], wbufs=[B("v_sb")])

                emit_ln1(0)
                for j in range(NTT):
                    emit_qk(j)
                    if j + 1 < NTT:
                        emit_ln1(j + 1)
                    emit_v(j)
                barrier()
                P.emit(nc, waited)
                P = Prog()

            with ExitStack() as ctxB2:
                wout_sb = ctxB2.enter_context(nc.sbuf_tensor("wout_sb", [128, KD * 1024], BF16))
                k_sb = [ctxB2.enter_context(nc.sbuf_tensor(f"k_sb{i}", [128, T], BF16))
                        for i in range(2)]
                q_sb = [ctxB2.enter_context(nc.sbuf_tensor(f"q_sb{i}", [128, KD * 512], BF16))
                        for i in range(2)]
                e_sb = [ctxB2.enter_context(nc.sbuf_tensor(f"e_sb{i}", [128, 1024], BF16))
                        for i in range(4)]
                o_sb = ctxB2.enter_context(nc.sbuf_tensor("o_sb", [128, KD * 512], BF16))
                af_sb = ctxB2.enter_context(nc.sbuf_tensor("af_sb", [128, KD * 512], BF16))
                r_rows = ctxB2.enter_context(nc.sbuf_tensor("r_rows", [1, 1024], F32))
                r_bf = ctxB2.enter_context(nc.sbuf_tensor("r_bf", [1, 1024], BF16))
                rb_sb = tmp2_sb

                sWo = DMA_SEMS["wo"]
                for oj in range(KD):
                    dma("gpsimd", sWo, wout_sb[:, oj * 1024:(oj + 1) * 1024], wout_s[oj],
                        wbufs=[B(f"wout{oj}")])
                for oj in range(KD):
                    B(f"wout{oj}").w = (sWo, sWo.n)

                pso = [ps4, ps5]
                psob = [B("ps4"), B("ps5")]

                def kload(pc):
                    kb = k_sb[pc % 2]
                    kbuf = B(f"k_sb{pc % 2}")
                    dma("sync", DMA_SEMS[f"kb{pc % 2}"], kb[:], k_sc[pc % KD], wbufs=[kbuf])

                def emit_inner(j, pc, depth):
                    kb = k_sb[pc % 2]
                    kbuf = B(f"k_sb{pc % 2}")
                    qb = q_sb[j % 2]
                    qbuf = B(f"q_sb{j % 2}")
                    qsl = qb[:, pc * 512:(pc + 1) * 512]
                    rot = [(psA, "psA0", "psA1"), (psB, "psB0", "psB1"),
                           (psC, "psC0", "psC1")][:depth]

                    def emit_S(tk):
                        psS, nA, nB = rot[tk % depth]
                        mm_group([(psS[:, 0:512], kb[0:64, tk * 128:(tk + 1) * 128],
                                   qsl[0:64, :])], B(nA), [kbuf, qbuf])
                        mm_group([(psS[:, 512:1024], kb[64:128, tk * 128:(tk + 1) * 128],
                                   qsl[64:128, :])], B(nB), [kbuf, qbuf])

                    def emit_exp(tk):
                        psS, nA, nB = rot[tk % depth]
                        eb = e_sb[tk % 4]
                        ebuf = B(f"e_sb{tk % 4}")
                        op("scalar", sACT,
                           lambda e, eb=eb, ps=psS: e.activation(
                               out=eb[:], in_=ps[:], func=AF.Exp, scale=1.0 / 64),
                           rbufs=[B(nA), B(nB)], wbufs=[ebuf])

                    def emit_PV(tk):
                        eb = e_sb[tk % 4]
                        ebuf = B(f"e_sb{tk % 4}")
                        for hh in range(2):
                            h = 2 * pc + hh
                            waits = list(ebuf.read_waits()) + B("v_sb").read_waits() \
                                + B("v_ones").read_waits()
                            if tk == 0:
                                waits += psob[hh].write_waits()
                            tok = P.add(
                                "tensor",
                                lambda e, o=pso[hh][0:65, :], l=v4[:, tk, h, :],
                                r=eb[:, hh * 512:(hh + 1) * 512],
                                st=(tk == 0), sp=(tk == TKT - 1): e.matmul(
                                    o, l, r, start=st, stop=sp, skip_group_check=True),
                                waits=waits, incs=[sPE])
                            ebuf.read(tok)
                            B("v_sb").read(tok)
                            B("v_ones").read(tok)
                            if tk == 0:
                                psob[hh].wrote(tok)
                            else:
                                psob[hh].w = tok

                    lead = depth - 1
                    for tk in range(lead):
                        emit_S(tk)
                        emit_exp(tk)
                    tk = 0
                    while tk < TKT:
                        for dd in (0, 1):
                            t2 = tk + dd + lead
                            if t2 < TKT:
                                emit_S(t2)
                                emit_exp(t2)
                        emit_PV(tk)
                        if tk + 1 < TKT:
                            emit_PV(tk + 1)
                        tk += 2

                def emit_epilogue(j, pc):
                    for hh in range(2):
                        rr = B(f"rrow{hh}")
                        op("scalar", sACT,
                           lambda e, hh=hh, ps=pso[hh]: e.activation(
                               out=r_rows[:, hh * 512:(hh + 1) * 512], in_=ps[64:65, :],
                               func=AF.Ln),
                           rbufs=[psob[hh]], wbufs=[rr])
                        op("scalar", sACT,
                           lambda e, hh=hh: e.activation(
                               out=r_rows[:, hh * 512:(hh + 1) * 512],
                               in_=r_rows[:, hh * 512:(hh + 1) * 512],
                               func=AF.Exp, scale=-1.0),
                           rbufs=[rr], wbufs=[rr])
                        bc_ps, bcb = (psC0, B("psC0")) if hh == 0 else (psC1, B("psC1"))
                        mm_group([(bc_ps[0:64, :], ones_row[:, 0:64],
                                   r_rows[:, hh * 512:(hh + 1) * 512])], bcb,
                                 [rr, B("ones")])
                        rbb = B("tmp2_sb")
                        op("vector", sDVE,
                           lambda e, bc=bc_ps: e.tensor_copy(out=rb_sb[0:64, :],
                                                             in_=bc[0:64, :]),
                           rbufs=[bcb], wbufs=[rbb])
                        op("vector", sDVE,
                           lambda e, o=o_sb[hh * 64:(hh + 1) * 64,
                                            pc * 512:(pc + 1) * 512],
                           ps=pso[hh]: e.tensor_tensor(
                               out=o, in0=ps[0:64, :], in1=rb_sb[0:64, :], op=ALU.mult),
                           rbufs=[psob[hh], rbb], wbufs=[B("o_sb")])

                def tail_outproj(j):
                    for oc in range(KD):
                        psx, psxb = (psC0, B("psC0")) if oc % 2 == 0 else (psC1, B("psC1"))
                        mm_group([(psx,
                                   wout_sb[:, oc * 1024 + p2 * 128: oc * 1024 + (p2 + 1) * 128],
                                   o_sb[:, p2 * 512:(p2 + 1) * 512]) for p2 in range(KD)],
                                 psxb, [B("o_sb"), B(f"wout{oc}")])
                        et = op("vector", sDVE,
                                lambda e, oc=oc, ps=psx: e.tensor_scalar(
                                    out=af_sb[:, oc * 512:(oc + 1) * 512], in0=ps,
                                    scalar1=bout_sb[:, oc:oc + 1], scalar2=None, op0=ALU.add),
                                rbufs=[psxb, B("bout")], wbufs=[B("af_sb")])
                        sqt = op("vector", sDVE,
                                 lambda e, oc=oc: e.tensor_tensor(
                                     out=sq_sb[:, oc * 512:(oc + 1) * 512],
                                     in0=af_sb[:, oc * 512:(oc + 1) * 512],
                                     in1=af_sb[:, oc * 512:(oc + 1) * 512], op=ALU.mult),
                                 rbufs=[], wbufs=[B("sq1")], extra_waits=[et])
                        B("af_sb").read(sqt)

                def tail_ln3(j):
                    xr, xrbuf = xt[0], B("xt0")
                    x1, x1buf = xt[1], B("xt1")
                    dma("sync", DMA_SEMS["xt0"], xr[:].rearrange("p (c t) -> p c t", t=512),
                        x_3d[:, :, j * 512:(j + 1) * 512], wbufs=[xrbuf])
                    layer_norm(af_sb, B("af_sb"), sq_sb[:], B("sq1"),
                               lambda c: (x1[:, c * 512:(c + 1) * 512], x1buf),
                               residual=(lambda c: xr[:, c * 512:(c + 1) * 512], xrbuf),
                               ps_cfg=(psC0, "psC0", "psC0_hi", psC0, "psC0", psC1, "psC1"),
                               zb="self", pre_done=True)
                    dma("gpsimd", DMA_SEMS["spx1"], x1_3d[:, :, j * 512:(j + 1) * 512],
                        x1[:].rearrange("p (c t) -> p c t", t=512), rbufs=[x1buf])
                    op("vector", sDVE,
                       lambda e: e.tensor_copy(out=lnb0[:], in_=x1[:]),
                       rbufs=[x1buf], wbufs=[B("lnb0")])
                    op("vector", sDVE,
                       lambda e: e.tensor_tensor(out=sq_sb[:], in0=x1[:], in1=x1[:],
                                                 op=ALU.mult),
                       rbufs=[x1buf], wbufs=[B("sq1")])

                def tail_ln2(j):
                    x1, x1buf = xt[1], B("xt1")
                    lnbuf2 = B("lnb0")
                    layer_norm(x1, x1buf, sq_sb[:], B("sq1"),
                               lambda c: (lnb0[:, c * 512:(c + 1) * 512], lnbuf2),
                               ps_cfg=(psC0, "psC0", "psC0_hi", psC0, "psC0", psC1, "psC1"),
                               zb=(lnb0[:], lnbuf2), pre_done=True)
                    dma("gpsimd", DMA_SEMS["spln2"], ln2_3d[:, :, j * 512:(j + 1) * 512],
                        lnb0[:].rearrange("p (c t) -> p c t", t=512), rbufs=[lnbuf2])

                tail = None
                for j in range(NTT):
                    qb = q_sb[j % 2]
                    qbuf = B(f"q_sb{j % 2}")
                    dma("sync", DMA_SEMS[f"qb{j % 2}"], qb[:].rearrange("p (pc t) -> p pc t", t=512),
                        q_3d[:, :, j * 512:(j + 1) * 512], wbufs=[qbuf])
                    if j == 0:
                        kload(0)
                    if tail is not None:
                        tail_outproj(tail)
                    for pc in range(KD):
                        kload(pc + 1)
                        depth = 2 if (tail is not None and pc < 4) else 3
                        emit_inner(j, pc, depth)
                        emit_epilogue(j, pc)
                        if tail is not None and pc == 1:
                            tail_ln3(tail)
                        if tail is not None and pc == 2:
                            tail_ln2(tail)
                            tail = None
                    tail = j
                tail_outproj(tail)
                tail_ln3(tail)
                tail_ln2(tail)
                barrier()
                P.emit(nc, waited)
                P = Prog()

        with ExitStack() as ctxC:
            ln2q = ctxC.enter_context(nc.sbuf_tensor("ln2q", [128, KD * 1024], BF16))
            h0_sb = ctxC.enter_context(nc.sbuf_tensor("h0_sb", [128, KF * 1024], BF16))
            h1o_sb = ctxC.enter_context(nc.sbuf_tensor("h1o_sb", [128, KF * 1024], BF16))
            stg = [ctxC.enter_context(nc.sbuf_tensor(f"stg{i}", [128, 4096], BF16))
                   for i in range(2)]
            ffn_sb = ctxC.enter_context(nc.sbuf_tensor("ffn_sb", [128, KD * 512], F32))
            a_sb = ctxC.enter_context(nc.sbuf_tensor("a_sb", [128, 1024], BF16))
            cs_sb = ctxC.enter_context(nc.sbuf_tensor("cs_sb", [128, 3 * 512], BF16))
            qs_sb = ctxC.enter_context(nc.sbuf_tensor("qs_sb", [128, 3 * 512], BF16))
            x1c = ctxC.enter_context(nc.sbuf_tensor("x1c", [128, 2 * 512], F32))

            for Q in range(NQ):
                lq = B("ln2q")
                if Q == 0:
                    dma("sync", DMA_SEMS["ln2q"], ln2q[:].rearrange("p (c t) -> p c t", t=1024),
                        ln2_3d[:, :, 0:1024], wbufs=[lq])
                for oj in range(KF):
                    sg = stg[oj % 2]
                    sgb = B(f"stg{oj % 2}a")
                    sgb2 = B(f"stg{oj % 2}b")
                    dma("gpsimd", DMA_SEMS[f"stg{oj % 2}a"], sg[:, 0:1024], w1_s[oj], wbufs=[sgb])
                    dma("gpsimd", DMA_SEMS[f"stg{oj % 2}b"], sg[:, 1024:2048], w2_s[oj], wbufs=[sgb2])
                    if oj % 2 == 0:
                        w1ps = [(psA0, B("psA0")), (psA1, B("psA1"))]
                        w2ps = [(psB0, B("psB0")), (psB1, B("psB1"))]
                    else:
                        w1ps = [(ps4[:], B("ps4")), (ps5[:], B("ps5"))]
                        w2ps = [(psC0, B("psC0")), (psC1, B("psC1"))]
                    mm_group2([(w1ps[0][0], w1ps[1][0], sg[:, k * 128:(k + 1) * 128],
                                ln2q[:, k * 1024: k * 1024 + 512],
                                ln2q[:, k * 1024 + 512: k * 1024 + 1024])
                               for k in range(KD)], w1ps[0][1], w1ps[1][1], [lq, sgb])
                    mm_group2([(w2ps[0][0], w2ps[1][0], sg[:, 1024 + k * 128: 1024 + (k + 1) * 128],
                                ln2q[:, k * 1024: k * 1024 + 512],
                                ln2q[:, k * 1024 + 512: k * 1024 + 1024])
                               for k in range(KD)], w2ps[0][1], w2ps[1][1], [lq, sgb2])
                    for tt in range(2):
                        ps_w1, pw1b = w1ps[tt]
                        ps_w2, pw2b = w2ps[tt]
                        ab = B("a_sb")
                        op("scalar", sACT,
                           lambda e, tt=tt, oj=oj, ps=ps_w1: e.activation(
                               out=a_sb[:, tt * 512:(tt + 1) * 512], in_=ps,
                               func=AF.Silu, scale=s1_sb[:, oj:oj + 1]),
                           rbufs=[pw1b, B("s1")], wbufs=[ab])
                        op("vector", sDVE,
                           lambda e, tt=tt, oj=oj, ps=ps_w2: e.scalar_tensor_tensor(
                               out=h0_sb[:, oj * 1024 + tt * 512: oj * 1024 + tt * 512 + 512],
                               in0=ps, scalar=s2_sb[:, oj:oj + 1],
                               in1=a_sb[:, tt * 512:(tt + 1) * 512],
                               op0=ALU.mult, op1=ALU.mult),
                           rbufs=[pw2b, ab, B("s2")], wbufs=[B("h0")])

                def h_stage(w_staged, scale_sb, scale_buf, in_sb, in_buf, out_sb, out_buf):
                    for oj in range(KF):
                        sg = stg[oj % 2]
                        sgb = B(f"stg{oj % 2}a")
                        sgb2 = B(f"stg{oj % 2}b")
                        dma("gpsimd", DMA_SEMS[f"stg{oj % 2}a"], sg[:, 0:2048],
                            w_staged[oj][:, 0:2048], wbufs=[sgb, sgb2])
                        dma("gpsimd", DMA_SEMS[f"stg{oj % 2}b"], sg[:, 2048:4096],
                            w_staged[oj][:, 2048:4096], wbufs=[sgb2])
                        hps = [[(psA0, B("psA0")), (psA1, B("psA1"))],
                               [(psB0, B("psB0")), (psB1, B("psB1"))]][oj % 2]
                        _, tokb = mm_group2(
                            [(hps[0][0], hps[1][0], sg[:, k * 128:(k + 1) * 128],
                              in_sb[:, k * 1024: k * 1024 + 512],
                              in_sb[:, k * 1024 + 512: k * 1024 + 1024])
                             for k in range(KF)], hps[0][1], hps[1][1], [in_buf, sgb],
                            at_waits={16: [sgb2.w]})
                        sgb2.read(tokb)
                        for tt in range(2):
                            psx, pxb = hps[tt]
                            op("scalar", sACT,
                               lambda e, oj=oj, tt=tt, ps=psx: e.activation(
                                   out=out_sb[:, oj * 1024 + tt * 512:
                                              oj * 1024 + tt * 512 + 512],
                                   in_=ps, func=AF.Gelu, scale=scale_sb[:, oj:oj + 1]),
                               rbufs=[pxb, scale_buf], wbufs=[out_buf])

                if Q + 1 < NQ:
                    dma("sync", DMA_SEMS["ln2q"],
                        ln2q[:].rearrange("p (c t) -> p c t", t=1024),
                        ln2_3d[:, :, (Q + 1) * 1024:(Q + 2) * 1024], wbufs=[lq])

                h_stage(h1_s, h1s_sb, B("h1s"), h0_sb, B("h0"), h1o_sb, B("h1o"))
                h_stage(h2_s, h2s_sb, B("h2s"), h1o_sb, B("h1o"), h0_sb, B("h0"))

                for tt in range(2):
                    ts = Q * 1024 + tt * 512
                    for c in (0, 1):
                        dma("sync", DMA_SEMS[f"xt{c}"], x1c[:, c * 512:(c + 1) * 512],
                            x1_sc[c][:, ts:ts + 512], wbufs=[B(f"x1c{c}")])
                    if tt == 0:
                        stat_ps, stat, stat2 = psA0, B("psA0"), B("psA0_hi")
                        bcm_ps, bcm, bcr_ps, bcr = psA1, B("psA1"), psB0, B("psB0")
                    else:
                        stat_ps, stat, stat2 = psC0, B("psC0"), B("psC0_hi")
                        bcm_ps, bcm, bcr_ps, bcr = psC0, B("psC0"), psC1, B("psC1")
                    stat_w = stat.write_waits()
                    stat2_w = stat2.write_waits()
                    sum_tok = sumsq_tok = None
                    csb = [B(f"cs{i}") for i in range(3)]
                    qsb = [B(f"qs{i}") for i in range(3)]
                    pend = {}

                    def emit_stats(oj):
                        nonlocal sum_tok, sumsq_tok
                        sl = oj % 3
                        ct, qt_ = pend.pop(oj)
                        sum_tok = P.add("tensor",
                                        lambda e, st=(oj == 0), sp=(oj == KD - 1), sps=stat_ps,
                                        sl=sl: e.matmul(
                                            sps[0:1, :], ones_b[:],
                                            cs_sb[:, sl * 512:(sl + 1) * 512],
                                            start=st, stop=sp, skip_group_check=True),
                                        waits=(stat_w if oj == 0 else []) + [ct],
                                        incs=[sPE] if oj == KD - 1 else [])
                        csb[sl].read(sum_tok)
                        sumsq_tok = P.add("tensor",
                                          lambda e, st=(oj == 0), sp=(oj == KD - 1), sps=stat_ps,
                                          sl=sl: e.matmul(
                                              sps[64:65, :], ones_b[:],
                                              qs_sb[:, sl * 512:(sl + 1) * 512],
                                              start=st, stop=sp, skip_group_check=True),
                                          waits=(stat2_w if oj == 0 else []) + [qt_],
                                          incs=[sPE] if oj == KD - 1 else [])
                        qsb[sl].read(sumsq_tok)

                    for oj in range(KD):
                        sg = stg[oj % 2]
                        sgb = B(f"stg{oj % 2}a")
                        sgb2 = B(f"stg{oj % 2}b")
                        dma("gpsimd", DMA_SEMS[f"stg{oj % 2}a"], sg[:, 0:2048],
                            w3_s[oj][:, 0:2048], wbufs=[sgb, sgb2])
                        dma("gpsimd", DMA_SEMS[f"stg{oj % 2}b"], sg[:, 2048:4096],
                            w3_s[oj][:, 2048:4096], wbufs=[sgb2])
                        psx, pxb = (ps4, B("ps4")) if oj % 2 == 0 else (ps5, B("ps5"))
                        w3tok = mm_group(
                            [(psx[:], sg[:, k * 128:(k + 1) * 128],
                              h0_sb[:, k * 1024 + tt * 512: k * 1024 + tt * 512 + 512])
                             for k in range(KF)], pxb, [B("h0"), sgb],
                            at_waits={16: [sgb2.w]})
                        sgb2.read(w3tok)
                        if oj - 2 in pend:
                            emit_stats(oj - 2)
                        sl = oj % 3
                        et = op("vector", sDVE,
                                lambda e, oj=oj, ps=psx: e.tensor_scalar(
                                    out=ffn_sb[:, oj * 512:(oj + 1) * 512], in0=ps[:],
                                    scalar1=w3s_sb[:, oj:oj + 1], scalar2=None, op0=ALU.mult),
                                rbufs=[pxb, B("w3s")], wbufs=[B("ffn")])
                        ct = op("vector", sDVE,
                                lambda e, oj=oj, sl=sl: e.tensor_copy(
                                    out=cs_sb[:, sl * 512:(sl + 1) * 512],
                                    in_=ffn_sb[:, oj * 512:(oj + 1) * 512]),
                                rbufs=[], wbufs=[csb[sl]], extra_waits=[et])
                        B("ffn").read(ct)
                        qt_ = op("vector", sDVE,
                                 lambda e, sl=sl: e.tensor_tensor(
                                     out=qs_sb[:, sl * 512:(sl + 1) * 512],
                                     in0=cs_sb[:, sl * 512:(sl + 1) * 512],
                                     in1=cs_sb[:, sl * 512:(sl + 1) * 512], op=ALU.mult),
                                 rbufs=[csb[sl]], wbufs=[qsb[sl]])
                        pend[oj] = (ct, qt_)
                    for oj in (KD - 2, KD - 1):
                        emit_stats(oj)
                    stat.wrote(sum_tok)
                    stat2.wrote(sumsq_tok)
                    rm, rv, rr = B("row_m"), B("row_v"), B("row_r")
                    op("vector", sDVE,
                       lambda e, ps=stat_ps: e.tensor_scalar(out=row_m[:], in0=ps[0:1, :],
                                                             scalar1=1.0 / D, scalar2=None,
                                                             op0=ALU.mult),
                       rbufs=[stat], wbufs=[rm])
                    op("vector", sDVE,
                       lambda e: e.tensor_tensor(out=row_v[:], in0=row_m[:], in1=row_m[:],
                                                 op=ALU.mult),
                       rbufs=[rm], wbufs=[rv])
                    op("vector", sDVE,
                       lambda e, ps=stat_ps: e.scalar_tensor_tensor(
                           out=row_v[:], in0=ps[64:65, :], scalar=1.0 / D, in1=row_v[:],
                           op0=ALU.mult, op1=ALU.subtract),
                       rbufs=[stat2, rv], wbufs=[rv])
                    op("vector", sDVE,
                       lambda e: e.tensor_scalar(out=row_v[:], in0=row_v[:], scalar1=EPS,
                                                 scalar2=None, op0=ALU.add),
                       rbufs=[rv], wbufs=[rv])
                    op("scalar", sACT,
                       lambda e: e.activation(out=row_r[:], in_=row_v[:], func=AF.Ln),
                       rbufs=[rv], wbufs=[rr])
                    op("scalar", sACT,
                       lambda e: e.activation(out=row_r[:], in_=row_r[:], func=AF.Exp,
                                              scale=-0.5),
                       rbufs=[rr], wbufs=[rr])
                    mm_group([(bcm_ps, ones_row[:], row_m[:])], bcm, [rm, B("ones")],
                             extra_waits=stat.write_waits() + stat2.write_waits())
                    mm_group([(bcr_ps, ones_row[:], row_r[:])], bcr, [rr, B("ones")])
                    tb, tb2 = B("tmp_sb"), B("tmp2_sb")
                    for c in range(KD):
                        xcb = B(f"x1c{c % 2}")
                        op("vector", sDVE,
                           lambda e, c=c, bp=bcm_ps: e.tensor_tensor(
                               out=tmp_sb[:], in0=ffn_sb[:, c * 512:(c + 1) * 512],
                               in1=bp, op=ALU.subtract),
                           rbufs=[B("ffn"), bcm], wbufs=[tb])
                        op("vector", sDVE,
                           lambda e, bp=bcr_ps: e.tensor_tensor(out=tmp2_sb[:], in0=tmp_sb[:],
                                                                in1=bp, op=ALU.mult),
                           rbufs=[tb, bcr], wbufs=[tb2])
                        op("vector", sDVE,
                           lambda e, c=c: e.tensor_tensor(
                               out=ffn_sb[:, c * 512:(c + 1) * 512], in0=tmp2_sb[:],
                               in1=x1c[:, (c % 2) * 512:(c % 2 + 1) * 512], op=ALU.add),
                           rbufs=[tb2, xcb], wbufs=[B("ffn")])
                        if c + 2 < KD:
                            dma("sync", DMA_SEMS[f"xt{c % 2}"], x1c[:, (c % 2) * 512:(c % 2 + 1) * 512],
                                x1_sc[c + 2][:, ts:ts + 512], wbufs=[xcb])
                    dma("sync", DMA_SEMS["spout"], out_3d[:, :, ts:ts + 512],
                        ffn_sb[:].rearrange("p (c t) -> p c t", t=512), rbufs=[B("ffn")])

            P.add("sync", lambda e: e.nop(),
                  waits=[(DMA_SEMS[nm], DMA_SEMS[nm].n)
                         for nm in ("spq", "spk", "spx1", "spln2", "spout")
                         if DMA_SEMS[nm].n > 0])
            P.emit(nc, waited)

    return nc


def _stage_weight(w):
    O, I = w.shape
    t = w.reshape(O // 128, 128, I // 128, 128)
    t = np.ascontiguousarray(t.transpose(0, 3, 2, 1))
    return t.reshape(O // 128, 128, I)


def _stage_vec(s):
    return np.ascontiguousarray(s.reshape(-1, 128).T)


def kernel(L0_attn_x, L0_attn_w_qkv, L0_attn_w_out, L0_attn_b_out,
           L0_ffn_w1_w, L0_ffn_w1_s, L0_ffn_w2_w, L0_ffn_w2_s,
           L0_ffn_h1_w, L0_ffn_h1_s, L0_ffn_h2_w, L0_ffn_h2_s,
           L0_ffn_w3_w, L0_ffn_w3_s):
    x = np.asarray(L0_attn_x, dtype=np.float32)
    Bn, T, _ = x.shape
    shared = {
        "wqkv_s": _stage_weight(np.asarray(L0_attn_w_qkv, np.float32)),
        "wout_s": _stage_weight(np.asarray(L0_attn_w_out, np.float32)),
        "bout_s": _stage_vec(np.asarray(L0_attn_b_out, np.float32)),
        "w1_s": _stage_weight(np.asarray(L0_ffn_w1_w, np.float32)),
        "w2_s": _stage_weight(np.asarray(L0_ffn_w2_w, np.float32)),
        "h1_s": _stage_weight(np.asarray(L0_ffn_h1_w, np.float32)),
        "h2_s": _stage_weight(np.asarray(L0_ffn_h2_w, np.float32)),
        "w3_s": _stage_weight(np.asarray(L0_ffn_w3_w, np.float32)),
        "s1_v": _stage_vec(np.asarray(L0_ffn_w1_s, np.float32)),
        "s2_v": _stage_vec(np.asarray(L0_ffn_w2_s, np.float32)),
        "h1s_v": _stage_vec(np.asarray(L0_ffn_h1_s, np.float32)),
        "h2s_v": _stage_vec(np.asarray(L0_ffn_h2_s, np.float32)),
        "w3s_v": _stage_vec(np.asarray(L0_ffn_w3_s, np.float32)),
    }
    in_maps = []
    for b in range(Bn):
        m = dict(shared)
        m["x_f"] = np.ascontiguousarray(x[b].T)
        in_maps.append(m)
    nc = build_program(T)
    res = run_bass_kernel_spmd(nc, in_maps, core_ids=list(range(NCORES)))
    out = np.stack([np.ascontiguousarray(r["out_f"].T) for r in res.results], axis=0)
    return out.astype(np.float32)


# version 42
# speedup vs baseline: 1.0311x; 1.0311x over previous
import numpy as np
from contextlib import ExitStack

import concourse.bass as bass
import concourse.mybir as mybir
from concourse.bass_utils import run_bass_kernel_spmd

F32 = mybir.dt.float32
BF16 = mybir.dt.bfloat16
AF = mybir.ActivationFunctionType
ALU = mybir.AluOpType

D = 1024
H = 16
FF = 4096
KD = D // 128
KF = FF // 128
NCORES = 8
EPS = 1e-5

ENGS = ("tensor", "scalar", "vector", "sync", "gpsimd")


class Sem:
    def __init__(self, ctx, nc, name, step=1):
        self.h = ctx.enter_context(nc.semaphore(name))
        self.n = 0
        self.step = step
        self.name = name


class Buf:

    def __init__(self):
        self.w = None
        self.rs = []

    def read_waits(self):
        return [self.w] if self.w else []

    def write_waits(self):
        return list(self.rs) + ([self.w] if self.w else [])

    def wrote(self, tok):
        self.w = tok
        self.rs = []

    def read(self, tok):
        if tok is not None:
            self.rs.append(tok)


class Prog:

    def __init__(self):
        self.ev = {e: [] for e in ENGS}

    def add(self, eng, fn, waits=(), incs=()):
        incs = tuple(incs)
        toks = []
        for s in incs:
            s.n += s.step
            toks.append(s.n)
        w = [(s, v) for (s, v) in waits if v is not None and v > 0]
        self.ev[eng].append((fn, w, incs, toks))
        if len(incs) == 1:
            return (incs[0], toks[0])
        return None

    def emit(self, nc, waited):
        with nc.Block() as block:
            for engn in ENGS:
                evs = self.ev[engn]
                if not evs:
                    continue

                def body(eng, evs=evs, wd=waited[engn]):
                    for fn, waits, incs, toks in evs:
                        for (s, v) in waits:
                            if wd.get(s.name, 0) < v:
                                eng.wait_ge(s.h, v)
                                wd[s.name] = v
                        ins = fn(eng)
                        for s, t in zip(incs, toks):
                            ins = ins.then_inc(s.h, s.step)

                getattr(block, engn)(body)


def build_program(T):
    NTT = T // 512
    NQ = T // 1024
    TKT = T // 128
    nc = bass.Bass("TRN2", target_bir_lowering=False)

    x_f = nc.dram_tensor("x_f", [D, T], F32, kind="ExternalInput").ap()
    wqkv_s = nc.dram_tensor("wqkv_s", [3 * KD, 128, D], F32, kind="ExternalInput").ap()
    wout_s = nc.dram_tensor("wout_s", [KD, 128, D], F32, kind="ExternalInput").ap()
    bout_s = nc.dram_tensor("bout_s", [128, KD], F32, kind="ExternalInput").ap()
    w1_s = nc.dram_tensor("w1_s", [KF, 128, D], F32, kind="ExternalInput").ap()
    w2_s = nc.dram_tensor("w2_s", [KF, 128, D], F32, kind="ExternalInput").ap()
    h1_s = nc.dram_tensor("h1_s", [KF, 128, FF], F32, kind="ExternalInput").ap()
    h2_s = nc.dram_tensor("h2_s", [KF, 128, FF], F32, kind="ExternalInput").ap()
    w3_s = nc.dram_tensor("w3_s", [KD, 128, FF], F32, kind="ExternalInput").ap()
    s1_v = nc.dram_tensor("s1_v", [128, KF], F32, kind="ExternalInput").ap()
    s2_v = nc.dram_tensor("s2_v", [128, KF], F32, kind="ExternalInput").ap()
    h1s_v = nc.dram_tensor("h1s_v", [128, KF], F32, kind="ExternalInput").ap()
    h2s_v = nc.dram_tensor("h2s_v", [128, KF], F32, kind="ExternalInput").ap()
    w3s_v = nc.dram_tensor("w3s_v", [128, KD], F32, kind="ExternalInput").ap()
    out_f = nc.dram_tensor("out_f", [D, T], F32, kind="ExternalOutput").ap()

    q_sc = nc.dram_tensor("q_sc", [KD, 128, T], BF16, kind="Internal").ap()
    k_sc = nc.dram_tensor("k_sc", [KD, 128, T], BF16, kind="Internal").ap()
    x1_sc = nc.dram_tensor("x1_sc", [KD, 128, T], F32, kind="Internal").ap()
    ln2_sc = nc.dram_tensor("ln2_sc", [KD, 128, T], BF16, kind="Internal").ap()

    x_3d = x_f.rearrange("(c p) t -> p c t", p=128)
    x1_3d = x1_sc.rearrange("c p t -> p c t")
    out_3d = out_f.rearrange("(c p) t -> p c t", p=128)
    q_3d = q_sc.rearrange("c p t -> p c t")
    k_3d = k_sc.rearrange("c p t -> p c t")
    ln2_3d = ln2_sc.rearrange("c p t -> p c t")

    with ExitStack() as octx:
        DMA_SEMS = {}
        for nm in ("cst", "wq", "wo", "xt0", "xt1", "qb0", "qb1", "kb0", "kb1",
                   "ln2q", "x1r", "stg0a", "stg0b", "stg1a", "stg1b", "stg2a", "stg2b",
                   "spq", "spk", "spx1", "spln2", "spout"):
            DMA_SEMS[nm] = Sem(octx, nc, "d_" + nm, 16)
        sPE = Sem(octx, nc, "sPE")
        sACT = Sem(octx, nc, "sACT")
        sDVE = Sem(octx, nc, "sDVE")
        sPh = Sem(octx, nc, "sPh")
        waited = {e: {} for e in ENGS}

        ones_f = octx.enter_context(nc.sbuf_tensor("ones_f", [128, 1], F32))
        ones_b = octx.enter_context(nc.sbuf_tensor("ones_b", [128, 1], BF16))
        ones_row = octx.enter_context(nc.sbuf_tensor("ones_row", [1, 128], F32))
        ones_row_b = octx.enter_context(nc.sbuf_tensor("ones_row_b", [1, 128], BF16))
        bout_sb = octx.enter_context(nc.sbuf_tensor("bout_sb", [128, KD], F32))
        s1_sb = octx.enter_context(nc.sbuf_tensor("s1_sb", [128, KF], F32))
        s2_sb = octx.enter_context(nc.sbuf_tensor("s2_sb", [128, KF], F32))
        h1s_sb = octx.enter_context(nc.sbuf_tensor("h1s_sb", [128, KF], F32))
        h2s_sb = octx.enter_context(nc.sbuf_tensor("h2s_sb", [128, KF], F32))
        w3s_sb = octx.enter_context(nc.sbuf_tensor("w3s_sb", [128, KD], F32))
        row_m = octx.enter_context(nc.sbuf_tensor("row_m", [1, 512], F32))
        row_v = octx.enter_context(nc.sbuf_tensor("row_v", [1, 512], F32))
        row_r = octx.enter_context(nc.sbuf_tensor("row_r", [1, 512], F32))
        tmp_sb = octx.enter_context(nc.sbuf_tensor("tmp_sb", [128, 512], F32))
        tmp2_sb = octx.enter_context(nc.sbuf_tensor("tmp2_sb", [128, 512], F32))

        psA = octx.enter_context(nc.psum_tensor("psA", [128, 1024], F32))
        psB = octx.enter_context(nc.psum_tensor("psB", [128, 1024], F32))
        psC = octx.enter_context(nc.psum_tensor("psC", [128, 1024], F32))
        ps4 = octx.enter_context(nc.psum_tensor("ps4", [128, 512], F32))
        ps5 = octx.enter_context(nc.psum_tensor("ps5", [128, 512], F32))
        psA0, psA1 = psA[:, 0:512], psA[:, 512:1024]
        psB0, psB1 = psB[:, 0:512], psB[:, 512:1024]
        psC0, psC1 = psC[:, 0:512], psC[:, 512:1024]

        bufs = {}

        def B(name):
            if name not in bufs:
                bufs[name] = Buf()
            return bufs[name]

        P = Prog()

        def dma(queue_eng, dsem, out_ap, in_ap, rbufs=(), wbufs=()):
            waits = []
            for b in rbufs:
                waits += b.read_waits()
            for b in wbufs:
                waits += b.write_waits()
            tok = P.add(queue_eng, lambda e, o=out_ap, i=in_ap: e.dma_start(out=o, in_=i),
                        waits=waits, incs=[dsem])
            for b in rbufs:
                b.read(tok)
            for b in wbufs:
                b.wrote(tok)
            return tok

        def op(eng, sem, fn, rbufs=(), wbufs=(), extra_waits=()):
            waits = list(extra_waits)
            for b in rbufs:
                waits += b.read_waits()
            for b in wbufs:
                waits += b.write_waits()
            tok = P.add(eng, fn, waits=waits, incs=[sem])
            for b in rbufs:
                b.read(tok)
            for b in wbufs:
                b.wrote(tok)
            return tok

        def mm_group(items, out_buf, rbufs, extra_waits=(), tp=None, at_waits=None):
            waits = list(extra_waits)
            for b in rbufs:
                waits += b.read_waits()
            waits += out_buf.write_waits()
            n = len(items)
            tok = None
            for i, (o, l, r) in enumerate(items):
                st, sp = (i == 0), (i == n - 1)
                w_i = list(waits) if i == 0 else []
                if at_waits and i in at_waits:
                    w_i += [w for w in at_waits[i] if w is not None]
                tok = P.add("tensor",
                            lambda e, o=o, l=l, r=r, st=st, sp=sp, tp=tp: e.matmul(
                                o, l, r, start=st, stop=sp, tile_position=tp),
                            waits=w_i, incs=[sPE] if sp else [])
            for b in rbufs:
                b.read(tok)
            out_buf.wrote(tok)
            return tok

        def mm_group2(items2, out_a, out_b, rbufs, extra_waits=(), at_waits=None):
            waits = list(extra_waits)
            for b in rbufs:
                waits += b.read_waits()
            waits += out_a.write_waits() + out_b.write_waits()
            n = len(items2)
            tok_a = tok_b = None
            for i, (oa, ob, l, ra, rb) in enumerate(items2):
                st, sp = (i == 0), (i == n - 1)
                w_i = list(waits) if i == 0 else []
                if at_waits and i in at_waits:
                    w_i += [w for w in at_waits[i] if w is not None]
                tok_a = P.add("tensor",
                              lambda e, o=oa, l=l, r=ra, st=st, sp=sp: e.matmul(
                                  o, l, r, start=st, stop=sp, skip_group_check=True),
                              waits=w_i,
                              incs=[sPE] if sp else [])
                tok_b = P.add("tensor",
                              lambda e, o=ob, l=l, r=rb, st=st, sp=sp: e.matmul(
                                  o, l, r, start=st, stop=sp, skip_group_check=True),
                              waits=(), incs=[sPE] if sp else [])
            for b in rbufs:
                b.read(tok_b)
            out_a.wrote(tok_a)
            out_b.wrote(tok_b)
            return tok_a, tok_b

        def barrier():
            dma_totals = [(sm, sm.n) for sm in DMA_SEMS.values() if sm.n > 0]
            for eng in ENGS:
                if eng == "sync":
                    P.add(eng, lambda e: e.nop(), waits=dma_totals, incs=[sPh])
                else:
                    P.add(eng, lambda e: e.nop(), incs=[sPh])
            tgt = sPh.n
            for eng in ENGS:
                P.add(eng, lambda e: e.nop(), waits=[(sPh, tgt)])

        def layer_norm(z_sb, z_buf, sq_ap, sq_buf, out_fn, residual=None, ps_cfg=None,
                       zb=None, pre_done=False):
            if ps_cfg is None:
                stat_ps, statB0, statB1, bcm_ps, bcmB, bcr_ps, bcrB = (
                    psA0, "psA0", "psA0_hi", psA1, "psA1", psB0, "psB0")
            else:
                stat_ps, statB0, statB1, bcm_ps, bcmB, bcr_ps, bcrB = ps_cfg
            sqb = sq_buf
            if not pre_done:
                op("scalar", sACT,
                   lambda e: e.activation(out=sq_ap, in_=z_sb[:], func=AF.Square),
                   rbufs=[z_buf], wbufs=[sqb])
            stat = B(statB0)
            if zb == "self":
                mm_group([(stat_ps[0:1, :], ones_b[:], z_sb[:, c * 512:(c + 1) * 512])
                          for c in range(KD)], stat, [z_buf, B("ones")])
            elif zb is not None:
                zb_ap, zb_buf = zb
                if not pre_done:
                    op("vector", sDVE,
                       lambda e: e.tensor_copy(out=zb_ap, in_=z_sb[:]),
                       rbufs=[z_buf], wbufs=[zb_buf])
                mm_group([(stat_ps[0:1, :], ones_b[:], zb_ap[:, c * 512:(c + 1) * 512])
                          for c in range(KD)], stat, [zb_buf, B("ones")])
            else:
                mm_group([(stat_ps[0:1, :], ones_f[:], z_sb[:, c * 512:(c + 1) * 512])
                          for c in range(KD)], stat, [z_buf, B("ones")])
            stat2 = B(statB1)
            mm_group([(stat_ps[64:65, :], ones_b[:], sq_ap[:, c * 512:(c + 1) * 512]) for c in range(KD)],
                     stat2, [sqb, B("ones")])
            rm, rv, rr = B("row_m"), B("row_v"), B("row_r")
            op("vector", sDVE,
               lambda e: e.tensor_scalar(out=row_m[:], in0=stat_ps[0:1, :], scalar1=1.0 / D,
                                         scalar2=None, op0=ALU.mult),
               rbufs=[stat], wbufs=[rm])
            op("vector", sDVE,
               lambda e: e.tensor_tensor(out=row_v[:], in0=row_m[:], in1=row_m[:], op=ALU.mult),
               rbufs=[rm], wbufs=[rv])
            op("vector", sDVE,
               lambda e: e.scalar_tensor_tensor(out=row_v[:], in0=stat_ps[64:65, :], scalar=1.0 / D,
                                                in1=row_v[:], op0=ALU.mult, op1=ALU.subtract),
               rbufs=[stat2, rv], wbufs=[rv])
            op("vector", sDVE,
               lambda e: e.tensor_scalar(out=row_v[:], in0=row_v[:], scalar1=EPS, scalar2=None,
                                         op0=ALU.add),
               rbufs=[rv], wbufs=[rv])
            op("scalar", sACT,
               lambda e: e.activation(out=row_r[:], in_=row_v[:], func=AF.Ln),
               rbufs=[rv], wbufs=[rr])
            op("scalar", sACT,
               lambda e: e.activation(out=row_r[:], in_=row_r[:], func=AF.Exp, scale=-0.5),
               rbufs=[rr], wbufs=[rr])
            bcm, bcr = B(bcmB), B(bcrB)
            mm_group([(bcm_ps, ones_row[:], row_m[:])], bcm, [rm, B("ones")],
                     extra_waits=stat.write_waits() + stat2.write_waits())
            mm_group([(bcr_ps, ones_row[:], row_r[:])], bcr, [rr, B("ones")])
            tb = B("tmp_sb")
            tb2 = B("tmp2_sb")
            for c in range(KD):
                zc = z_sb[:, c * 512:(c + 1) * 512]
                op("vector", sDVE,
                   lambda e, zc=zc: e.tensor_tensor(out=tmp_sb[:], in0=zc, in1=bcm_ps,
                                                    op=ALU.subtract),
                   rbufs=[z_buf, bcm], wbufs=[tb])
                out_ap, out_b = out_fn(c)
                if residual is None:
                    op("vector", sDVE,
                       lambda e, o=out_ap: e.tensor_tensor(out=o, in0=tmp_sb[:], in1=bcr_ps,
                                                           op=ALU.mult),
                       rbufs=[tb, bcr], wbufs=[out_b])
                else:
                    res_fn, res_buf = residual
                    op("vector", sDVE,
                       lambda e: e.tensor_tensor(out=tmp2_sb[:], in0=tmp_sb[:], in1=bcr_ps,
                                                 op=ALU.mult),
                       rbufs=[tb, bcr], wbufs=[tb2])
                    op("vector", sDVE,
                       lambda e, o=out_ap, r=res_fn(c): e.tensor_tensor(out=o, in0=tmp2_sb[:],
                                                                        in1=r, op=ALU.add),
                       rbufs=[tb2, res_buf], wbufs=[out_b])

        with ExitStack() as ctxA:
            v_sb = ctxA.enter_context(nc.sbuf_tensor("v_sb", [128, TKT * H * 65], BF16))
            lnb0 = ctxA.enter_context(nc.sbuf_tensor("lnb0", [128, KD * 512], BF16))
            sq_sb = ctxA.enter_context(nc.sbuf_tensor("sq_sb", [128, KD * 512], BF16))
            xt = [ctxA.enter_context(nc.sbuf_tensor(f"xt{i}", [128, KD * 512], F32))
                  for i in range(2)]
            v4 = v_sb[:].rearrange("p (tb h j) -> p tb h j", h=H, j=65)

            with ExitStack() as ctxB1:
                wqkv_sb = ctxB1.enter_context(nc.sbuf_tensor("wqkv_sb", [128, 24 * 1024], BF16))
                lnb = [lnb0, ctxB1.enter_context(nc.sbuf_tensor("lnb1", [128, KD * 512], BF16))]
                qt2 = [ctxB1.enter_context(nc.sbuf_tensor(f"qt{i}", [128, KD * 512], BF16))
                       for i in range(2)]
                kt_s = ctxB1.enter_context(nc.sbuf_tensor("kt", [128, KD * 512], BF16))
                wv = wqkv_sb[:].rearrange("p (oj x) -> p oj x", x=1024)

                op("vector", sDVE, lambda e: e.memset(ones_f[:], 1.0), wbufs=[B("ones")])
                op("vector", sDVE, lambda e: e.memset(ones_b[:], 1.0), wbufs=[B("ones")])
                op("vector", sDVE, lambda e: e.memset(ones_row[:], 1.0), wbufs=[B("ones")])
                op("vector", sDVE, lambda e: e.memset(ones_row_b[:], 1.0), wbufs=[B("ones")])
                op("vector", sDVE, lambda e: e.memset(v4[:, :, :, 64:65], 1.0), wbufs=[B("v_ones")])
                sCst = DMA_SEMS["cst"]
                dma("sync", sCst, bout_sb[:], bout_s, wbufs=[B("bout")])
                dma("sync", sCst, s1_sb[:], s1_v, wbufs=[B("s1")])
                dma("sync", sCst, s2_sb[:], s2_v, wbufs=[B("s2")])
                dma("sync", sCst, h1s_sb[:], h1s_v, wbufs=[B("h1s")])
                dma("sync", sCst, h2s_sb[:], h2s_v, wbufs=[B("h2s")])
                dma("sync", sCst, w3s_sb[:], w3s_v, wbufs=[B("w3s")])
                for nm in ("bout", "s1", "s2", "h1s", "h2s", "w3s"):
                    B(nm).w = (sCst, sCst.n)
                sWq = DMA_SEMS["wq"]
                for oj in range(24):
                    dma("gpsimd", sWq, wqkv_sb[:, oj * 1024:(oj + 1) * 1024], wqkv_s[oj],
                        wbufs=[B(f"wqkv{oj}")])
                for oj in range(24):
                    B(f"wqkv{oj}").w = (sWq, sWq.n)

                def emit_ln1(j):
                    xb = xt[j % 2]
                    xbuf = B(f"xt{j % 2}")
                    lb = lnb[j % 2]
                    lnbuf = B(f"lnb{j % 2}")
                    dma("sync", DMA_SEMS[f"xt{j % 2}"], xb[:].rearrange("p (c t) -> p c t", t=512),
                        x_3d[:, :, j * 512:(j + 1) * 512], wbufs=[xbuf])
                    layer_norm(xb, xbuf, sq_sb[:], B("sq1"),
                               lambda c: (lb[:, c * 512:(c + 1) * 512], lnbuf),
                               zb=(lb[:], lnbuf))

                def emit_qk(j):
                    qt, kt = qt2[j % 2], kt_s
                    qtb, ktb = B(f"qt{j % 2}"), B("kt")
                    lb = lnb[j % 2]
                    lnbuf = B(f"lnb{j % 2}")
                    for oc in range(16):
                        psx, psxb = (ps4, B("ps4")) if oc % 2 == 0 else (ps5, B("ps5"))
                        mm_group([(psx[:],
                                   wqkv_sb[:, oc * 1024 + k * 128: oc * 1024 + (k + 1) * 128],
                                   lb[:, k * 512:(k + 1) * 512]) for k in range(KD)],
                                 psxb, [lnbuf, B(f"wqkv{oc}")])
                        dst, dbuf = (qt, qtb) if oc < 8 else (kt, ktb)
                        pc = oc % 8
                        op("scalar", sACT,
                           lambda e, d=dst, pc=pc, ps=psx: e.copy(
                               out=d[:, pc * 512:(pc + 1) * 512], in_=ps[:]),
                           rbufs=[psxb], wbufs=[dbuf])
                    dma("gpsimd", DMA_SEMS["spq"], q_3d[:, :, j * 512:(j + 1) * 512],
                        qt[:].rearrange("p (pc t) -> p pc t", t=512), rbufs=[qtb])
                    dma("gpsimd", DMA_SEMS["spk"], k_3d[:, :, j * 512:(j + 1) * 512],
                        kt[:].rearrange("p (pc t) -> p pc t", t=512), rbufs=[ktb])

                def emit_v(j):
                    lb = lnb[j % 2]
                    lnbuf = B(f"lnb{j % 2}")
                    for t4 in range(4):
                        tb_ix = j * 4 + t4
                        for g in range(2):
                            psv, psvb = (psC0, B("psC0")) if g == 0 else (psC1, B("psC1"))
                            items = []
                            for k in range(KD):
                                lhsT = lb[:, k * 512 + t4 * 128: k * 512 + t4 * 128 + 128]
                                rhs = wv[:, 16 + 4 * g:16 + 4 * g + 4, k * 128:(k + 1) * 128]
                                items.append((psv[:], lhsT, rhs))
                            mm_group(items, psvb,
                                     [lnbuf] + [B(f"wqkv{16 + 4 * g + i}") for i in range(4)])
                            op("vector", sDVE,
                               lambda e, o=v4[:, tb_ix, 8 * g:8 * g + 8, 0:64], ps=psv:
                               e.tensor_copy(out=o, in_=ps),
                               rbufs=[psvb], wbufs=[B("v_sb")])

                emit_ln1(0)
                for j in range(NTT):
                    emit_qk(j)
                    if j + 1 < NTT:
                        emit_ln1(j + 1)
                    emit_v(j)
                barrier()
                P.emit(nc, waited)
                P = Prog()

            with ExitStack() as ctxB2:
                wout_sb = ctxB2.enter_context(nc.sbuf_tensor("wout_sb", [128, KD * 1024], BF16))
                k_sb = [ctxB2.enter_context(nc.sbuf_tensor(f"k_sb{i}", [128, T], BF16))
                        for i in range(2)]
                q_sb = [ctxB2.enter_context(nc.sbuf_tensor(f"q_sb{i}", [128, KD * 512], BF16))
                        for i in range(2)]
                e_sb = [ctxB2.enter_context(nc.sbuf_tensor(f"e_sb{i}", [128, 1024], BF16))
                        for i in range(4)]
                o_sb = ctxB2.enter_context(nc.sbuf_tensor("o_sb", [128, KD * 512], BF16))
                af_sb = ctxB2.enter_context(nc.sbuf_tensor("af_sb", [128, KD * 512], BF16))
                r_rows = ctxB2.enter_context(nc.sbuf_tensor("r_rows", [1, 1024], F32))
                r_bf = ctxB2.enter_context(nc.sbuf_tensor("r_bf", [1, 1024], BF16))
                rb_sb = tmp2_sb

                sWo = DMA_SEMS["wo"]
                for oj in range(KD):
                    dma("gpsimd", sWo, wout_sb[:, oj * 1024:(oj + 1) * 1024], wout_s[oj],
                        wbufs=[B(f"wout{oj}")])
                for oj in range(KD):
                    B(f"wout{oj}").w = (sWo, sWo.n)

                pso = [ps4, ps5]
                psob = [B("ps4"), B("ps5")]

                def kload(pc):
                    kb = k_sb[pc % 2]
                    kbuf = B(f"k_sb{pc % 2}")
                    dma("sync", DMA_SEMS[f"kb{pc % 2}"], kb[:], k_sc[pc % KD], wbufs=[kbuf])

                def emit_inner(j, pc, depth):
                    kb = k_sb[pc % 2]
                    kbuf = B(f"k_sb{pc % 2}")
                    qb = q_sb[j % 2]
                    qbuf = B(f"q_sb{j % 2}")
                    qsl = qb[:, pc * 512:(pc + 1) * 512]
                    rot = [(psA, "psA0", "psA1"), (psB, "psB0", "psB1"),
                           (psC, "psC0", "psC1")][:depth]

                    def emit_S(tk):
                        psS, nA, nB = rot[tk % depth]
                        mm_group([(psS[:, 0:512], kb[0:64, tk * 128:(tk + 1) * 128],
                                   qsl[0:64, :])], B(nA), [kbuf, qbuf])
                        mm_group([(psS[:, 512:1024], kb[64:128, tk * 128:(tk + 1) * 128],
                                   qsl[64:128, :])], B(nB), [kbuf, qbuf])

                    def emit_exp(tk):
                        psS, nA, nB = rot[tk % depth]
                        eb = e_sb[tk % 4]
                        ebuf = B(f"e_sb{tk % 4}")
                        op("scalar", sACT,
                           lambda e, eb=eb, ps=psS: e.activation(
                               out=eb[:], in_=ps[:], func=AF.Exp, scale=1.0 / 64),
                           rbufs=[B(nA), B(nB)], wbufs=[ebuf])

                    def emit_PV(tk):
                        eb = e_sb[tk % 4]
                        ebuf = B(f"e_sb{tk % 4}")
                        for hh in range(2):
                            h = 2 * pc + hh
                            waits = list(ebuf.read_waits()) + B("v_sb").read_waits() \
                                + B("v_ones").read_waits()
                            if tk == 0:
                                waits += psob[hh].write_waits()
                            tok = P.add(
                                "tensor",
                                lambda e, o=pso[hh][0:65, :], l=v4[:, tk, h, :],
                                r=eb[:, hh * 512:(hh + 1) * 512],
                                st=(tk == 0), sp=(tk == TKT - 1): e.matmul(
                                    o, l, r, start=st, stop=sp, skip_group_check=True),
                                waits=waits, incs=[sPE])
                            ebuf.read(tok)
                            B("v_sb").read(tok)
                            B("v_ones").read(tok)
                            if tk == 0:
                                psob[hh].wrote(tok)
                            else:
                                psob[hh].w = tok

                    lead = depth - 1
                    for tk in range(lead):
                        emit_S(tk)
                        emit_exp(tk)
                    tk = 0
                    while tk < TKT:
                        for dd in (0, 1):
                            t2 = tk + dd + lead
                            if t2 < TKT:
                                emit_S(t2)
                                emit_exp(t2)
                        emit_PV(tk)
                        if tk + 1 < TKT:
                            emit_PV(tk + 1)
                        tk += 2

                def emit_epilogue(j, pc):
                    for hh in range(2):
                        rr = B(f"rrow{hh}")
                        op("scalar", sACT,
                           lambda e, hh=hh, ps=pso[hh]: e.activation(
                               out=r_rows[:, hh * 512:(hh + 1) * 512], in_=ps[64:65, :],
                               func=AF.Ln),
                           rbufs=[psob[hh]], wbufs=[rr])
                        op("scalar", sACT,
                           lambda e, hh=hh: e.activation(
                               out=r_rows[:, hh * 512:(hh + 1) * 512],
                               in_=r_rows[:, hh * 512:(hh + 1) * 512],
                               func=AF.Exp, scale=-1.0),
                           rbufs=[rr], wbufs=[rr])
                        bc_ps, bcb = (psC0, B("psC0")) if hh == 0 else (psC1, B("psC1"))
                        mm_group([(bc_ps[0:64, :], ones_row[:, 0:64],
                                   r_rows[:, hh * 512:(hh + 1) * 512])], bcb,
                                 [rr, B("ones")])
                        rbb = B("tmp2_sb")
                        op("vector", sDVE,
                           lambda e, bc=bc_ps: e.tensor_copy(out=rb_sb[0:64, :],
                                                             in_=bc[0:64, :]),
                           rbufs=[bcb], wbufs=[rbb])
                        op("vector", sDVE,
                           lambda e, o=o_sb[hh * 64:(hh + 1) * 64,
                                            pc * 512:(pc + 1) * 512],
                           ps=pso[hh]: e.tensor_tensor(
                               out=o, in0=ps[0:64, :], in1=rb_sb[0:64, :], op=ALU.mult),
                           rbufs=[psob[hh], rbb], wbufs=[B("o_sb")])

                def tail_outproj(j):
                    for oc in range(KD):
                        psx, psxb = (psC0, B("psC0")) if oc % 2 == 0 else (psC1, B("psC1"))
                        mm_group([(psx,
                                   wout_sb[:, oc * 1024 + p2 * 128: oc * 1024 + (p2 + 1) * 128],
                                   o_sb[:, p2 * 512:(p2 + 1) * 512]) for p2 in range(KD)],
                                 psxb, [B("o_sb"), B(f"wout{oc}")])
                        et = op("vector", sDVE,
                                lambda e, oc=oc, ps=psx: e.tensor_scalar(
                                    out=af_sb[:, oc * 512:(oc + 1) * 512], in0=ps,
                                    scalar1=bout_sb[:, oc:oc + 1], scalar2=None, op0=ALU.add),
                                rbufs=[psxb, B("bout")], wbufs=[B("af_sb")])
                        sqt = op("vector", sDVE,
                                 lambda e, oc=oc: e.tensor_tensor(
                                     out=sq_sb[:, oc * 512:(oc + 1) * 512],
                                     in0=af_sb[:, oc * 512:(oc + 1) * 512],
                                     in1=af_sb[:, oc * 512:(oc + 1) * 512], op=ALU.mult),
                                 rbufs=[], wbufs=[B("sq1")], extra_waits=[et])
                        B("af_sb").read(sqt)

                def tail_ln3(j):
                    xr, xrbuf = xt[0], B("xt0")
                    x1, x1buf = xt[1], B("xt1")
                    dma("sync", DMA_SEMS["xt0"], xr[:].rearrange("p (c t) -> p c t", t=512),
                        x_3d[:, :, j * 512:(j + 1) * 512], wbufs=[xrbuf])
                    layer_norm(af_sb, B("af_sb"), sq_sb[:], B("sq1"),
                               lambda c: (x1[:, c * 512:(c + 1) * 512], x1buf),
                               residual=(lambda c: xr[:, c * 512:(c + 1) * 512], xrbuf),
                               ps_cfg=(psC0, "psC0", "psC0_hi", psC0, "psC0", psC1, "psC1"),
                               zb="self", pre_done=True)
                    dma("gpsimd", DMA_SEMS["spx1"], x1_3d[:, :, j * 512:(j + 1) * 512],
                        x1[:].rearrange("p (c t) -> p c t", t=512), rbufs=[x1buf])
                    op("vector", sDVE,
                       lambda e: e.tensor_copy(out=lnb0[:], in_=x1[:]),
                       rbufs=[x1buf], wbufs=[B("lnb0")])
                    op("vector", sDVE,
                       lambda e: e.tensor_tensor(out=sq_sb[:], in0=x1[:], in1=x1[:],
                                                 op=ALU.mult),
                       rbufs=[x1buf], wbufs=[B("sq1")])

                def tail_ln2(j):
                    x1, x1buf = xt[1], B("xt1")
                    lnbuf2 = B("lnb0")
                    layer_norm(x1, x1buf, sq_sb[:], B("sq1"),
                               lambda c: (lnb0[:, c * 512:(c + 1) * 512], lnbuf2),
                               ps_cfg=(psC0, "psC0", "psC0_hi", psC0, "psC0", psC1, "psC1"),
                               zb=(lnb0[:], lnbuf2), pre_done=True)
                    dma("gpsimd", DMA_SEMS["spln2"], ln2_3d[:, :, j * 512:(j + 1) * 512],
                        lnb0[:].rearrange("p (c t) -> p c t", t=512), rbufs=[lnbuf2])

                tail = None
                for j in range(NTT):
                    qb = q_sb[j % 2]
                    qbuf = B(f"q_sb{j % 2}")
                    dma("sync", DMA_SEMS[f"qb{j % 2}"], qb[:].rearrange("p (pc t) -> p pc t", t=512),
                        q_3d[:, :, j * 512:(j + 1) * 512], wbufs=[qbuf])
                    if j == 0:
                        kload(0)
                    if tail is not None:
                        tail_outproj(tail)
                    for pc in range(KD):
                        kload(pc + 1)
                        depth = 2 if (tail is not None and pc < 4) else 3
                        emit_inner(j, pc, depth)
                        emit_epilogue(j, pc)
                        if tail is not None and pc == 1:
                            tail_ln3(tail)
                        if tail is not None and pc == 2:
                            tail_ln2(tail)
                            tail = None
                    tail = j
                tail_outproj(tail)
                tail_ln3(tail)
                tail_ln2(tail)
                barrier()
                P.emit(nc, waited)
                P = Prog()

        with ExitStack() as ctxC:
            ln2q = ctxC.enter_context(nc.sbuf_tensor("ln2q", [128, KD * 1024], BF16))
            h0_sb = ctxC.enter_context(nc.sbuf_tensor("h0_sb", [128, KF * 1024], BF16))
            h1o_sb = ctxC.enter_context(nc.sbuf_tensor("h1o_sb", [128, KF * 1024], BF16))
            stg = [ctxC.enter_context(nc.sbuf_tensor(f"stg{i}", [128, 4096], BF16))
                   for i in range(3)]
            ffn_sb = ctxC.enter_context(nc.sbuf_tensor("ffn_sb", [128, KD * 512], F32))
            a_sb = ctxC.enter_context(nc.sbuf_tensor("a_sb", [128, 1024], BF16))
            cs_sb = ctxC.enter_context(nc.sbuf_tensor("cs_sb", [128, 3 * 512], BF16))
            qs_sb = ctxC.enter_context(nc.sbuf_tensor("qs_sb", [128, 3 * 512], BF16))
            x1c = ctxC.enter_context(nc.sbuf_tensor("x1c", [128, 2 * 512], F32))

            for Q in range(NQ):
                lq = B("ln2q")
                if Q == 0:
                    dma("sync", DMA_SEMS["ln2q"], ln2q[:].rearrange("p (c t) -> p c t", t=1024),
                        ln2_3d[:, :, 0:1024], wbufs=[lq])
                for oj in range(KF):
                    sg = stg[oj % 2]
                    sgb = B(f"stg{oj % 2}a")
                    sgb2 = B(f"stg{oj % 2}b")
                    dma("gpsimd", DMA_SEMS[f"stg{oj % 2}a"], sg[:, 0:1024], w1_s[oj], wbufs=[sgb])
                    dma("gpsimd", DMA_SEMS[f"stg{oj % 2}b"], sg[:, 1024:2048], w2_s[oj], wbufs=[sgb2])
                    if oj % 2 == 0:
                        w1ps = [(psA0, B("psA0")), (psA1, B("psA1"))]
                        w2ps = [(psB0, B("psB0")), (psB1, B("psB1"))]
                    else:
                        w1ps = [(ps4[:], B("ps4")), (ps5[:], B("ps5"))]
                        w2ps = [(psC0, B("psC0")), (psC1, B("psC1"))]
                    mm_group2([(w1ps[0][0], w1ps[1][0], sg[:, k * 128:(k + 1) * 128],
                                ln2q[:, k * 1024: k * 1024 + 512],
                                ln2q[:, k * 1024 + 512: k * 1024 + 1024])
                               for k in range(KD)], w1ps[0][1], w1ps[1][1], [lq, sgb])
                    mm_group2([(w2ps[0][0], w2ps[1][0], sg[:, 1024 + k * 128: 1024 + (k + 1) * 128],
                                ln2q[:, k * 1024: k * 1024 + 512],
                                ln2q[:, k * 1024 + 512: k * 1024 + 1024])
                               for k in range(KD)], w2ps[0][1], w2ps[1][1], [lq, sgb2])
                    for tt in range(2):
                        ps_w1, pw1b = w1ps[tt]
                        ps_w2, pw2b = w2ps[tt]
                        ab = B("a_sb")
                        op("scalar", sACT,
                           lambda e, tt=tt, oj=oj, ps=ps_w1: e.activation(
                               out=a_sb[:, tt * 512:(tt + 1) * 512], in_=ps,
                               func=AF.Silu, scale=s1_sb[:, oj:oj + 1]),
                           rbufs=[pw1b, B("s1")], wbufs=[ab])
                        op("vector", sDVE,
                           lambda e, tt=tt, oj=oj, ps=ps_w2: e.scalar_tensor_tensor(
                               out=h0_sb[:, oj * 1024 + tt * 512: oj * 1024 + tt * 512 + 512],
                               in0=ps, scalar=s2_sb[:, oj:oj + 1],
                               in1=a_sb[:, tt * 512:(tt + 1) * 512],
                               op0=ALU.mult, op1=ALU.mult),
                           rbufs=[pw2b, ab, B("s2")], wbufs=[B("h0")])

                def h_stage(w_staged, scale_sb, scale_buf, in_sb, in_buf, out_sb, out_buf):
                    for oj in range(KF):
                        sl = oj % 3
                        sg = stg[sl]
                        sgb = B(f"stg{sl}a")
                        dma("gpsimd", DMA_SEMS[f"stg{sl}a"], sg[:], w_staged[oj],
                            wbufs=[sgb, B(f"stg{sl}b")])
                        hps = [[(psA0, B("psA0")), (psA1, B("psA1"))],
                               [(psB0, B("psB0")), (psB1, B("psB1"))]][oj % 2]
                        mm_group2([(hps[0][0], hps[1][0], sg[:, k * 128:(k + 1) * 128],
                                    in_sb[:, k * 1024: k * 1024 + 512],
                                    in_sb[:, k * 1024 + 512: k * 1024 + 1024])
                                   for k in range(KF)], hps[0][1], hps[1][1], [in_buf, sgb])
                        for tt in range(2):
                            psx, pxb = hps[tt]
                            op("scalar", sACT,
                               lambda e, oj=oj, tt=tt, ps=psx: e.activation(
                                   out=out_sb[:, oj * 1024 + tt * 512:
                                              oj * 1024 + tt * 512 + 512],
                                   in_=ps, func=AF.Gelu, scale=scale_sb[:, oj:oj + 1]),
                               rbufs=[pxb, scale_buf], wbufs=[out_buf])

                if Q + 1 < NQ:
                    dma("sync", DMA_SEMS["ln2q"],
                        ln2q[:].rearrange("p (c t) -> p c t", t=1024),
                        ln2_3d[:, :, (Q + 1) * 1024:(Q + 2) * 1024], wbufs=[lq])

                h_stage(h1_s, h1s_sb, B("h1s"), h0_sb, B("h0"), h1o_sb, B("h1o"))
                h_stage(h2_s, h2s_sb, B("h2s"), h1o_sb, B("h1o"), h0_sb, B("h0"))

                for tt in range(2):
                    ts = Q * 1024 + tt * 512
                    for c in (0, 1):
                        dma("sync", DMA_SEMS[f"xt{c}"], x1c[:, c * 512:(c + 1) * 512],
                            x1_sc[c][:, ts:ts + 512], wbufs=[B(f"x1c{c}")])
                    if tt == 0:
                        stat_ps, stat, stat2 = psA0, B("psA0"), B("psA0_hi")
                        bcm_ps, bcm, bcr_ps, bcr = psA1, B("psA1"), psB0, B("psB0")
                    else:
                        stat_ps, stat, stat2 = psC0, B("psC0"), B("psC0_hi")
                        bcm_ps, bcm, bcr_ps, bcr = psC0, B("psC0"), psC1, B("psC1")
                    stat_w = stat.write_waits()
                    stat2_w = stat2.write_waits()
                    sum_tok = sumsq_tok = None
                    csb = [B(f"cs{i}") for i in range(3)]
                    qsb = [B(f"qs{i}") for i in range(3)]
                    pend = {}

                    def emit_stats(oj):
                        nonlocal sum_tok, sumsq_tok
                        sl = oj % 3
                        ct, qt_ = pend.pop(oj)
                        sum_tok = P.add("tensor",
                                        lambda e, st=(oj == 0), sp=(oj == KD - 1), sps=stat_ps,
                                        sl=sl: e.matmul(
                                            sps[0:1, :], ones_b[:],
                                            cs_sb[:, sl * 512:(sl + 1) * 512],
                                            start=st, stop=sp, skip_group_check=True),
                                        waits=(stat_w if oj == 0 else []) + [ct],
                                        incs=[sPE] if oj == KD - 1 else [])
                        csb[sl].read(sum_tok)
                        sumsq_tok = P.add("tensor",
                                          lambda e, st=(oj == 0), sp=(oj == KD - 1), sps=stat_ps,
                                          sl=sl: e.matmul(
                                              sps[64:65, :], ones_b[:],
                                              qs_sb[:, sl * 512:(sl + 1) * 512],
                                              start=st, stop=sp, skip_group_check=True),
                                          waits=(stat2_w if oj == 0 else []) + [qt_],
                                          incs=[sPE] if oj == KD - 1 else [])
                        qsb[sl].read(sumsq_tok)

                    for oj in range(KD):
                        sl = oj % 3
                        sg = stg[sl]
                        sgb = B(f"stg{sl}a")
                        dma("gpsimd", DMA_SEMS[f"stg{sl}a"], sg[:], w3_s[oj],
                            wbufs=[sgb, B(f"stg{sl}b")])
                        psx, pxb = (ps4, B("ps4")) if oj % 2 == 0 else (ps5, B("ps5"))
                        mm_group([(psx[:], sg[:, k * 128:(k + 1) * 128],
                                   h0_sb[:, k * 1024 + tt * 512: k * 1024 + tt * 512 + 512])
                                  for k in range(KF)], pxb, [B("h0"), sgb])
                        if oj - 2 in pend:
                            emit_stats(oj - 2)
                        sl = oj % 3
                        et = op("vector", sDVE,
                                lambda e, oj=oj, ps=psx: e.tensor_scalar(
                                    out=ffn_sb[:, oj * 512:(oj + 1) * 512], in0=ps[:],
                                    scalar1=w3s_sb[:, oj:oj + 1], scalar2=None, op0=ALU.mult),
                                rbufs=[pxb, B("w3s")], wbufs=[B("ffn")])
                        ct = op("vector", sDVE,
                                lambda e, oj=oj, sl=sl: e.tensor_copy(
                                    out=cs_sb[:, sl * 512:(sl + 1) * 512],
                                    in_=ffn_sb[:, oj * 512:(oj + 1) * 512]),
                                rbufs=[], wbufs=[csb[sl]], extra_waits=[et])
                        B("ffn").read(ct)
                        qt_ = op("vector", sDVE,
                                 lambda e, sl=sl: e.tensor_tensor(
                                     out=qs_sb[:, sl * 512:(sl + 1) * 512],
                                     in0=cs_sb[:, sl * 512:(sl + 1) * 512],
                                     in1=cs_sb[:, sl * 512:(sl + 1) * 512], op=ALU.mult),
                                 rbufs=[csb[sl]], wbufs=[qsb[sl]])
                        pend[oj] = (ct, qt_)
                    for oj in (KD - 2, KD - 1):
                        emit_stats(oj)
                    stat.wrote(sum_tok)
                    stat2.wrote(sumsq_tok)
                    rm, rv, rr = B("row_m"), B("row_v"), B("row_r")
                    op("vector", sDVE,
                       lambda e, ps=stat_ps: e.tensor_scalar(out=row_m[:], in0=ps[0:1, :],
                                                             scalar1=1.0 / D, scalar2=None,
                                                             op0=ALU.mult),
                       rbufs=[stat], wbufs=[rm])
                    op("vector", sDVE,
                       lambda e: e.tensor_tensor(out=row_v[:], in0=row_m[:], in1=row_m[:],
                                                 op=ALU.mult),
                       rbufs=[rm], wbufs=[rv])
                    op("vector", sDVE,
                       lambda e, ps=stat_ps: e.scalar_tensor_tensor(
                           out=row_v[:], in0=ps[64:65, :], scalar=1.0 / D, in1=row_v[:],
                           op0=ALU.mult, op1=ALU.subtract),
                       rbufs=[stat2, rv], wbufs=[rv])
                    op("vector", sDVE,
                       lambda e: e.tensor_scalar(out=row_v[:], in0=row_v[:], scalar1=EPS,
                                                 scalar2=None, op0=ALU.add),
                       rbufs=[rv], wbufs=[rv])
                    op("scalar", sACT,
                       lambda e: e.activation(out=row_r[:], in_=row_v[:], func=AF.Ln),
                       rbufs=[rv], wbufs=[rr])
                    op("scalar", sACT,
                       lambda e: e.activation(out=row_r[:], in_=row_r[:], func=AF.Exp,
                                              scale=-0.5),
                       rbufs=[rr], wbufs=[rr])
                    mm_group([(bcm_ps, ones_row[:], row_m[:])], bcm, [rm, B("ones")],
                             extra_waits=stat.write_waits() + stat2.write_waits())
                    mm_group([(bcr_ps, ones_row[:], row_r[:])], bcr, [rr, B("ones")])
                    tb, tb2 = B("tmp_sb"), B("tmp2_sb")
                    for c in range(KD):
                        xcb = B(f"x1c{c % 2}")
                        op("vector", sDVE,
                           lambda e, c=c, bp=bcm_ps: e.tensor_tensor(
                               out=tmp_sb[:], in0=ffn_sb[:, c * 512:(c + 1) * 512],
                               in1=bp, op=ALU.subtract),
                           rbufs=[B("ffn"), bcm], wbufs=[tb])
                        op("vector", sDVE,
                           lambda e, bp=bcr_ps: e.tensor_tensor(out=tmp2_sb[:], in0=tmp_sb[:],
                                                                in1=bp, op=ALU.mult),
                           rbufs=[tb, bcr], wbufs=[tb2])
                        op("vector", sDVE,
                           lambda e, c=c: e.tensor_tensor(
                               out=ffn_sb[:, c * 512:(c + 1) * 512], in0=tmp2_sb[:],
                               in1=x1c[:, (c % 2) * 512:(c % 2 + 1) * 512], op=ALU.add),
                           rbufs=[tb2, xcb], wbufs=[B("ffn")])
                        if c + 2 < KD:
                            dma("sync", DMA_SEMS[f"xt{c % 2}"], x1c[:, (c % 2) * 512:(c % 2 + 1) * 512],
                                x1_sc[c + 2][:, ts:ts + 512], wbufs=[xcb])
                    dma("sync", DMA_SEMS["spout"], out_3d[:, :, ts:ts + 512],
                        ffn_sb[:].rearrange("p (c t) -> p c t", t=512), rbufs=[B("ffn")])

            P.add("sync", lambda e: e.nop(),
                  waits=[(DMA_SEMS[nm], DMA_SEMS[nm].n)
                         for nm in ("spq", "spk", "spx1", "spln2", "spout")
                         if DMA_SEMS[nm].n > 0])
            P.emit(nc, waited)

    return nc


def _stage_weight(w):
    O, I = w.shape
    t = w.reshape(O // 128, 128, I // 128, 128)
    t = np.ascontiguousarray(t.transpose(0, 3, 2, 1))
    return t.reshape(O // 128, 128, I)


def _stage_vec(s):
    return np.ascontiguousarray(s.reshape(-1, 128).T)


def kernel(L0_attn_x, L0_attn_w_qkv, L0_attn_w_out, L0_attn_b_out,
           L0_ffn_w1_w, L0_ffn_w1_s, L0_ffn_w2_w, L0_ffn_w2_s,
           L0_ffn_h1_w, L0_ffn_h1_s, L0_ffn_h2_w, L0_ffn_h2_s,
           L0_ffn_w3_w, L0_ffn_w3_s):
    x = np.asarray(L0_attn_x, dtype=np.float32)
    Bn, T, _ = x.shape
    shared = {
        "wqkv_s": _stage_weight(np.asarray(L0_attn_w_qkv, np.float32)),
        "wout_s": _stage_weight(np.asarray(L0_attn_w_out, np.float32)),
        "bout_s": _stage_vec(np.asarray(L0_attn_b_out, np.float32)),
        "w1_s": _stage_weight(np.asarray(L0_ffn_w1_w, np.float32)),
        "w2_s": _stage_weight(np.asarray(L0_ffn_w2_w, np.float32)),
        "h1_s": _stage_weight(np.asarray(L0_ffn_h1_w, np.float32)),
        "h2_s": _stage_weight(np.asarray(L0_ffn_h2_w, np.float32)),
        "w3_s": _stage_weight(np.asarray(L0_ffn_w3_w, np.float32)),
        "s1_v": _stage_vec(np.asarray(L0_ffn_w1_s, np.float32)),
        "s2_v": _stage_vec(np.asarray(L0_ffn_w2_s, np.float32)),
        "h1s_v": _stage_vec(np.asarray(L0_ffn_h1_s, np.float32)),
        "h2s_v": _stage_vec(np.asarray(L0_ffn_h2_s, np.float32)),
        "w3s_v": _stage_vec(np.asarray(L0_ffn_w3_s, np.float32)),
    }
    in_maps = []
    for b in range(Bn):
        m = dict(shared)
        m["x_f"] = np.ascontiguousarray(x[b].T)
        in_maps.append(m)
    nc = build_program(T)
    res = run_bass_kernel_spmd(nc, in_maps, core_ids=list(range(NCORES)))
    out = np.stack([np.ascontiguousarray(r["out_f"].T) for r in res.results], axis=0)
    return out.astype(np.float32)
